# version 01 (final)
# baseline (speedup 1.0000x reference)
import numpy as np
import ml_dtypes
from contextlib import ExitStack

import concourse.bass as bass
import concourse.mybir as mybir
from concourse.bass_utils import run_bass_kernel_spmd
from concourse.alu_op_type import AluOpType

F32 = mybir.dt.float32
F32R = mybir.dt.float32r
BF16 = mybir.dt.bfloat16
AF = mybir.ActivationFunctionType
BF = ml_dtypes.bfloat16

NCORES = 8
D = 2048
NT = 8192
TPC = NT // NCORES
TC = 512
NCH = TPC // TC
NKT = D // 128
INTER = 8192
NOT_I = INTER // 128

C1 = 1.0 / (64.0 * float(np.sqrt(2048.0)))
C2 = 1.0 / (64.0 * float(np.sqrt(8192.0)))
C3 = 1.0 / float(np.sqrt(8192.0))
L0_SM_SCALE = float(128.0 ** -0.5)
L1_SM_SCALE = 0.125
NEG = -1.0e30

ENGINES = ("sync", "tensor", "vector", "scalar", "gpsimd")


def _had(n):
    h = np.array([[1.0]], dtype=np.float32)
    while h.shape[0] < n:
        h = np.block([[h, h], [h, -h]])
    return h.astype(np.float32)


class P:

    def __init__(self, name):
        self.nc = bass.Bass("TRN2", target_bir_lowering=False, name=name,
                            dynamic_dma_scratch_size=2048)
        self.sched = {e: [] for e in ENGINES}
        self.total = 0
        self._uid = 0
        self._consts = {}

    def fconst(self, val):
        if val in self._consts:
            return self._consts[val]
        t = self.sb((128, 1))
        self.op("vector", lambda e, a=t.ap(), v=float(val): e.memset(a, v), inc=1)
        self._consts[val] = t.ap()
        return t.ap()

    def uid(self):
        self._uid += 1
        return self._uid

    def sb(self, shape, dtype=F32):
        return self.nc.alloc_sbuf_tensor(f"sb{self.uid()}", shape, dtype)

    def ps(self, shape=(128, 512), dtype=F32):
        return self.nc.alloc_psum_tensor(f"ps{self.uid()}", list(shape), dtype)

    def din(self, name, shape, dtype=F32):
        return self.nc.dram_tensor(name, list(shape), dtype, kind="ExternalInput").ap()

    def dout(self, name, shape, dtype=F32):
        return self.nc.dram_tensor(name, list(shape), dtype, kind="ExternalOutput").ap()

    def mark(self):
        return self.total

    def op(self, eng, fn, inc=1, wait=None):
        w = self.total if wait is None else wait
        self.total += inc
        self.sched[eng].append((fn, w, inc))

    def dma(self, out, in_, wait=None):
        self.op("sync", lambda e, o=out, i=in_: e.dma_start(out=o, in_=i),
                inc=16, wait=wait)

    def dma_burst(self, transfers):
        w = self.mark()
        for (o, i) in transfers:
            self.dma(o, i, wait=w)

    def mm_group(self, out_ps, pairs):
        w = self.mark()
        n = len(pairs)
        for i, (lt, rh) in enumerate(pairs):
            st = (i == 0)
            sp = (i == n - 1)
            inc = 1 if sp else 0
            self.op("tensor",
                    lambda e, o=out_ps, l=lt, r=rh, st=st, sp=sp:
                        e.matmul(o, l, r, start=st, stop=sp),
                    inc=inc, wait=w)

    def build(self):
        nc = self.nc
        stack = ExitStack()
        S = stack.enter_context(nc.semaphore("S"))
        blk = stack.enter_context(nc.Block())
        total = self.total
        for ename in ENGINES:
            items = self.sched[ename]
            if not items and ename != "sync":
                continue

            def body(eng, items=items):
                lastw = 0
                for (fn, w, inc) in items:
                    if w > lastw:
                        eng.wait_ge(S, w)
                        lastw = w
                    ins = fn(eng)
                    if inc > 0:
                        ins.then_inc(S, inc)
                eng.wait_ge(S, total)

            getattr(blk, ename)(body)
        stack.close()
        return nc


def bc(ap):
    return ap



def emit_ln(p, x_tiles, ch, ps_s, ps_q, ps_mb, ps_rb, ones128, sq_sb, st_sb):
    cs = slice(ch * TC, (ch + 1) * TC)
    for kt in range(NKT):
        xa = x_tiles[kt].ap()[:, cs]
        self_w = p.mark()
        p.op("tensor", lambda e, o=ps_s.ap()[0:1, :], l=bc(ones128.ap()), r=bc(xa),
             st=(kt == 0), sp=(kt == NKT - 1):
             e.matmul(o, l, r, start=st, stop=sp), inc=1)
        p.op("scalar", lambda e, o=sq_sb.ap(), i=xa:
             e.activation(o, i, AF.Square), inc=1)
        p.op("tensor", lambda e, o=ps_q.ap()[0:1, :], l=bc(ones128.ap()), r=bc(sq_sb.ap()),
             st=(kt == 0), sp=(kt == NKT - 1):
             e.matmul(o, l, r, start=st, stop=sp), inc=1)
    m = st_sb[0].ap()
    v = st_sb[1].ap()
    r = st_sb[2].ap()
    m2 = r
    p.op("vector", lambda e, o=m, i=ps_s.ap()[0:1, :]:
         e.tensor_scalar_mul(o, i, 1.0 / 2048.0), inc=1)
    p.op("vector", lambda e, o=v, i=ps_q.ap()[0:1, :]:
         e.tensor_scalar_mul(o, i, 1.0 / 2048.0), inc=1)
    p.op("vector", lambda e, o=m2, a=m, b=m:
         e.tensor_tensor(o, a, b, AluOpType.mult), inc=1)
    p.op("vector", lambda e, o=v, a=v, b=m2:
         e.tensor_tensor(o, a, b, AluOpType.subtract), inc=1)
    eps = p.fconst(1.0e-5)
    p.op("scalar", lambda e, o=r, i=v, b=eps[0:1, :]:
         e.activation(o, i, AF.Sqrt, bias=b), inc=1)
    p.op("vector", lambda e, o=r, i=r: e.reciprocal(o, i), inc=1)
    ones1 = emit_ln.ones1
    p.op("tensor", lambda e, o=ps_mb.ap(), l=bc(ones1.ap()), r=bc(m):
         e.matmul(o, l, r, start=True, stop=True), inc=1)
    p.op("tensor", lambda e, o=ps_rb.ap(), l=bc(ones1.ap()), r=bc(r):
         e.matmul(o, l, r, start=True, stop=True), inc=1)



def build_launch_w(conv_cols):
    p = P("lw")
    wup = p.din("wup", (16, 131072))
    wgate = p.din("wgate", (16, 131072))
    wdn = p.din("wdn", (64, 32768))
    w2 = p.din("w2", (64, 32768))
    h16d = p.din("h16", (16, 16))
    h64d = p.din("h64", (64, 64))
    cin = p.din("cin", (128, conv_cols))
    oup = p.dout("oup", (16, 131072), BF16)
    ogate = p.dout("ogate", (16, 131072), BF16)
    odn = p.dout("odn", (64, 32768), BF16)
    o2 = p.dout("o2", (64, 32768), BF16)
    cout = p.dout("cout", (128, conv_cols), BF16)

    h16 = p.sb((16, 16))
    h64 = p.sb((64, 64))
    p.dma_burst([(h16.ap(), h16d), (h64.ap(), h64d)])
    rsb = p.sb((128, 512))
    osb = p.sb((128, 512), BF16)
    ps0 = p.ps()

    for (src, dst, hsb, K) in ((wup, oup, h16, 16), (wgate, ogate, h16, 16),
                               (wdn, odn, h64, 64), (w2, o2, h64, 64)):
        ncols = src.shape[1]
        for c in range(ncols // 512):
            cs = slice(c * 512, (c + 1) * 512)
            p.dma(rsb.ap()[0:K, :], src[:, cs])
            p.op("tensor", lambda e, o=ps0.ap()[0:K, :], l=bc(hsb.ap()),
                 r=bc(rsb.ap()[0:K, :]):
                 e.matmul(o, l, r, start=True, stop=True), inc=1)
            p.op("scalar", lambda e, o=osb.ap()[0:K, :], i=ps0.ap()[0:K, :]:
                 e.copy(o, i), inc=1)
            p.dma(dst[:, cs], osb.ap()[0:K, :])
    for c in range(conv_cols // 512):
        cs = slice(c * 512, (c + 1) * 512)
        p.dma(rsb.ap(), cin[:, cs])
        p.op("scalar", lambda e, o=osb.ap(), i=rsb.ap(): e.copy(o, i), inc=1)
        p.dma(cout[:, cs], osb.ap())
    return p.build()



def build_launch_a():
    p = P("la")
    xt = p.din("xt", (D, TPC))
    w4 = {}
    bt = {}
    blo = {}
    a4 = {}
    for nm in ("q", "k", "v"):
        w4[nm] = p.din(f"w4{nm}", (16, 16, 128, 128))
        bt[nm] = p.din(f"bt{nm}", (128, 16))
        blo[nm] = p.din(f"blo{nm}", (16, 128, 16))
        a4[nm] = p.din(f"a4{nm}", (16, 16, 128))
    ones128d = p.din("ones128", (128, 1))
    ones1d = p.din("ones1", (1, 128))
    outs = {nm: p.dout(f"o{nm}", (D, TPC)) for nm in ("q", "k", "v")}

    x_tiles = [p.sb((128, TPC)) for _ in range(NKT)]
    p.dma_burst([(x_tiles[kt].ap(), xt[kt * 128:(kt + 1) * 128, :])
                 for kt in range(NKT)])
    ones128 = p.sb((128, 1))
    ones1 = p.sb((1, 128))
    p.dma_burst([(ones128.ap(), ones128d), (ones1.ap(), ones1d)])
    emit_ln.ones1 = ones1

    sq_sb = p.sb((128, 512))
    st_sb = tuple(p.sb((1, 512)) for _ in range(3))
    ps_s, ps_q, ps_mb, ps_rb = p.ps(), p.ps(), p.ps(), p.ps()
    ps_u, ps_p = p.ps(), p.ps()

    xn_tiles = [p.sb((128, TPC)) for _ in range(NKT)]
    for ch in range(NCH):
        cs = slice(ch * TC, (ch + 1) * TC)
        emit_ln(p, x_tiles, ch, ps_s, ps_q, ps_mb, ps_rb, ones128, sq_sb, st_sb)
        for kt in range(NKT):
            p.op("vector", lambda e, o=xn_tiles[kt].ap()[:, cs],
                 a=x_tiles[kt].ap()[:, cs], b=ps_mb.ap():
                 e.tensor_tensor(o, a, b, AluOpType.subtract), inc=1)
            p.op("vector", lambda e, o=xn_tiles[kt].ap()[:, cs],
                 a=xn_tiles[kt].ap()[:, cs], b=ps_rb.ap():
                 e.tensor_tensor(o, a, b, AluOpType.mult), inc=1)

    wsl = p.sb((128, 2048))
    bsb = p.sb((128, 16))
    bpt = p.sb((128, 256))
    apt = p.sb((16, 2048))
    usb = p.sb((16, TPC))
    osb = p.sb((128, 512))
    for nm in ("q", "k", "v"):
        p.dma_burst([(bsb.ap(), bt[nm])]
                    + [(bpt.ap()[:, kt * 16:(kt + 1) * 16], blo[nm][kt])
                       for kt in range(NKT)]
                    + [(apt.ap()[:, ot * 128:(ot + 1) * 128], a4[nm][ot])
                       for ot in range(NKT)])
        for ch in range(NCH):
            cs = slice(ch * TC, (ch + 1) * TC)
            p.mm_group(ps_u.ap()[0:16, :],
                       [(bc(bpt.ap()[:, kt * 16:(kt + 1) * 16]),
                         bc(xn_tiles[kt].ap()[:, cs])) for kt in range(NKT)])
            p.op("vector", lambda e, o=usb.ap()[:, cs], i=ps_u.ap()[0:16, :]:
                 e.tensor_scalar_mul(o, i, 0.5), inc=1)
        for ot in range(NKT):
            p.dma_burst([(wsl.ap()[:, kt * 128:(kt + 1) * 128], w4[nm][ot, kt])
                         for kt in range(NKT)])
            for ch in range(NCH):
                cs = slice(ch * TC, (ch + 1) * TC)
                pairs = [(bc(wsl.ap()[:, kt * 128:(kt + 1) * 128]),
                          bc(xn_tiles[kt].ap()[:, cs])) for kt in range(NKT)]
                pairs.append((bc(apt.ap()[:, ot * 128:(ot + 1) * 128]),
                              bc(usb.ap()[:, cs])))
                p.mm_group(ps_p.ap(), pairs)
                p.op("vector", lambda e, o=osb.ap(), i=ps_p.ap(),
                     s=bsb.ap()[:, ot:ot + 1]:
                     e.tensor_scalar_add(o, i, s), inc=1)
                p.dma(outs[nm][ot * 128:(ot + 1) * 128, cs], osb.ap())
    return p.build()



def build_launch_b():
    p = P("lb")
    L = 4096
    qd = [p.din(f"q{u}", (128, L)) for u in range(4)]
    kd = [p.din(f"k{u}", (128, L)) for u in range(4)]
    vd = [p.din(f"v{u}", (L, 128)) for u in range(4)]
    ones128d = p.din("ones128", (128, 1))
    ones1d = p.din("ones1", (1, 128))
    co = [p.dout(f"c{u}", (128, L), BF16) for u in range(4)]

    ones128 = p.sb((128, 1))
    ones1 = p.sb((1, 128))
    p.dma_burst([(ones128.ap(), ones128d), (ones1.ap(), ones1d)])
    q_sb = p.sb((128, L))
    k_sb = p.sb((128, L))
    v_sb = p.sb((128, 32 * 128))
    pt_sb = p.sb((128, 512))
    rs_sb = p.sb((1, 512))
    cx_sb = p.sb((128, 512))
    cxb_sb = p.sb((128, 512), BF16)
    ps_S, ps_sum, ps_ctx, ps_bb = p.ps(), p.ps(), p.ps(), p.ps()

    for u in range(4):
        p.dma_burst([(q_sb.ap(), qd[u]), (k_sb.ap(), kd[u])]
                    + [(v_sb.ap()[:, jt * 128:(jt + 1) * 128],
                        vd[u][jt * 128:(jt + 1) * 128, :]) for jt in range(32)])
        for ic in range(8):
            ics = slice(ic * 512, (ic + 1) * 512)
            for jt in range(32):
                p.op("tensor", lambda e, o=ps_S.ap(),
                     l=bc(k_sb.ap()[:, jt * 128:(jt + 1) * 128]),
                     r=bc(q_sb.ap()[:, ics]):
                     e.matmul(o, l, r, start=True, stop=True), inc=1)
                p.op("scalar", lambda e, o=pt_sb.ap(), i=ps_S.ap(),
                     s=p.fconst(L0_SM_SCALE):
                     e.activation(o, i, AF.Exp, scale=s), inc=1)
                p.op("tensor", lambda e, o=ps_sum.ap()[0:1, :],
                     l=bc(ones128.ap()), r=bc(pt_sb.ap()),
                     st=(jt == 0), sp=(jt == 31):
                     e.matmul(o, l, r, start=st, stop=sp), inc=1)
                p.op("tensor", lambda e, o=ps_ctx.ap(),
                     l=bc(v_sb.ap()[:, jt * 128:(jt + 1) * 128]), r=bc(pt_sb.ap()),
                     st=(jt == 0), sp=(jt == 31):
                     e.matmul(o, l, r, start=st, stop=sp), inc=1)
            p.op("vector", lambda e, o=rs_sb.ap(), i=ps_sum.ap()[0:1, :]:
                 e.reciprocal(o, i), inc=1)
            p.op("tensor", lambda e, o=ps_bb.ap(), l=bc(ones1.ap()), r=bc(rs_sb.ap()):
                 e.matmul(o, l, r, start=True, stop=True), inc=1)
            p.op("vector", lambda e, o=cx_sb.ap(), i=ps_ctx.ap():
                 e.tensor_copy(o, i), inc=1)
            p.op("vector", lambda e, o=cxb_sb.ap(), a=cx_sb.ap(), b=ps_bb.ap():
                 e.tensor_tensor(o, a, b, AluOpType.mult), inc=1)
            p.dma(co[u][:, ics], cxb_sb.ap())
    return p.build()



def build_launch_c():
    p = P("lc")
    xt = p.din("xt", (D, TPC))
    ctxt = p.din("ctxt", (D, TPC), BF16)
    wo4 = p.din("wo4", (16, 16, 128, 128), BF16)
    bot = p.din("bot", (128, 16))
    bolo = p.din("bolo", (16, 128, 16), BF16)
    ao4 = p.din("ao4", (16, 16, 128), BF16)
    su_ug = p.din("su_ug", (128, 16))
    su_dp = p.din("su_dp", (128, 64))
    ws_up = p.din("ws_up", (128, 64))
    ws_gate = p.din("ws_gate", (128, 64))
    ws_dp = p.din("ws_dp", (128, 16))
    wup4 = p.din("wup4", (64, 16, 128, 128), BF16)
    wgate4 = p.din("wgate4", (64, 16, 128, 128), BF16)
    wdn4 = p.din("wdn4", (16, 64, 128, 128), BF16)
    h128bd = p.din("h128b", (128, 128), BF16)
    ones128d = p.din("ones128", (128, 1))
    ones1d = p.din("ones1", (1, 128))
    wq14 = p.din("wq14", (16, 16, 128, 128), BF16)
    wk14 = p.din("wk14", (4, 16, 128, 128), BF16)
    wv14 = p.din("wv14", (4, 16, 128, 128), BF16)
    cosd = p.din("cosd", (128, TPC))
    sind = p.din("sind", (128, TPC))
    permd = p.din("permd", (128, 128))
    x2o = p.dout("x2o", (D, TPC))
    q1o = p.dout("q1o", (D, TPC))
    k1o = p.dout("k1o", (512, TPC))
    v1o = p.dout("v1o", (512, TPC))

    x_tiles = [p.sb((128, TPC)) for _ in range(NKT)]
    p.dma_burst([(x_tiles[kt].ap(), xt[kt * 128:(kt + 1) * 128, :])
                 for kt in range(NKT)])
    ones128 = p.sb((128, 1))
    ones1 = p.sb((1, 128))
    h128b = p.sb((128, 128), BF16)
    perm = p.sb((128, 128))
    cos_sb = p.sb((128, TPC))
    sin_sb = p.sb((128, TPC))
    p.dma_burst([(ones128.ap(), ones128d), (ones1.ap(), ones1d),
                 (h128b.ap(), h128bd), (perm.ap(), permd),
                 (cos_sb.ap(), cosd), (sin_sb.ap(), sind)])
    emit_ln.ones1 = ones1

    bo_sb = p.sb((128, 16))
    bpt = p.sb((128, 256), BF16)
    apt = p.sb((16, 2048), BF16)
    suug = p.sb((128, 16))
    sudp = p.sb((128, 64))
    wsup = p.sb((128, 64))
    wsgt = p.sb((128, 64))
    wsdp = p.sb((128, 16))
    p.dma_burst([(bo_sb.ap(), bot)]
                + [(bpt.ap()[:, kt * 16:(kt + 1) * 16], bolo[kt]) for kt in range(16)]
                + [(apt.ap()[:, ot * 128:(ot + 1) * 128], ao4[ot]) for ot in range(16)]
                + [(suug.ap(), su_ug), (sudp.ap(), su_dp), (wsup.ap(), ws_up),
                   (wsgt.ap(), ws_gate), (wsdp.ap(), ws_dp)])
    p.op("vector", lambda e, o=suug.ap(), i=suug.ap():
         e.tensor_scalar_mul(o, i, C1), inc=1)
    p.op("vector", lambda e, o=sudp.ap(), i=sudp.ap():
         e.tensor_scalar_mul(o, i, C2), inc=1)
    p.op("vector", lambda e, o=wsup.ap(), i=wsup.ap():
         e.tensor_scalar_mul(o, i, 64.0), inc=1)
    p.op("vector", lambda e, o=wsgt.ap(), i=wsgt.ap():
         e.tensor_scalar_mul(o, i, 64.0), inc=1)
    p.op("vector", lambda e, o=wsdp.ap(), i=wsdp.ap():
         e.tensor_scalar_mul(o, i, 64.0), inc=1)

    U1 = p.sb((128, NOT_I * 512), BF16)
    ctx_tiles = [U1.ap()[:, kt * TPC:(kt + 1) * TPC] for kt in range(NKT)]
    p.dma_burst([(ctx_tiles[kt], ctxt[kt * 128:(kt + 1) * 128, :])
                 for kt in range(NKT)])
    U2 = p.sb((128, NKT * 1024), BF16)

    sq_sb = p.sb((128, 512))
    st_sb = tuple(p.sb((1, 512)) for _ in range(3))
    ps_s, ps_q, ps_mb, ps_rb = p.ps(), p.ps(), p.ps(), p.ps()
    ps_a, ps_b, ps_c, ps_d = p.ps(), p.ps(), p.ps(), p.ps()

    wslo = p.sb((128, 2048), BF16)
    usb = p.sb((16, TPC))
    tsb = p.sb((128, 512))

    for ch in range(NCH):
        cs = slice(ch * TC, (ch + 1) * TC)
        p.mm_group(ps_a.ap()[0:16, :],
                   [(bpt.ap()[:, kt * 16:(kt + 1) * 16],
                     ctx_tiles[kt][:, cs]) for kt in range(NKT)])
        p.op("vector", lambda e, o=usb.ap()[:, cs], i=ps_a.ap()[0:16, :]:
             e.tensor_scalar_mul(o, i, 0.5), inc=1)
    usbb = p.sb((16, TPC), BF16)
    p.op("scalar", lambda e, o=usbb.ap(), i=usb.ap(): e.copy(o, i), inc=1)
    for ot in range(NKT):
        p.dma_burst([(wslo.ap()[:, kt * 128:(kt + 1) * 128], wo4[ot, kt])
                     for kt in range(NKT)])
        for ch in range(NCH):
            cs = slice(ch * TC, (ch + 1) * TC)
            pairs = [(wslo.ap()[:, kt * 128:(kt + 1) * 128],
                      ctx_tiles[kt][:, cs]) for kt in range(NKT)]
            pairs.append((apt.ap()[:, ot * 128:(ot + 1) * 128], usbb.ap()[:, cs]))
            p.mm_group(ps_b.ap(), pairs)
            p.op("vector", lambda e, o=tsb.ap(), i=ps_b.ap(),
                 s=bo_sb.ap()[:, ot:ot + 1]:
                 e.tensor_scalar_add(o, i, s), inc=1)
            p.op("vector", lambda e, o=x_tiles[ot].ap()[:, cs],
                 a=x_tiles[ot].ap()[:, cs], b=tsb.ap():
                 e.tensor_tensor(o, a, b, AluOpType.add), inc=1)

    u_tiles = [U2.ap()[:, kt * 1024:kt * 1024 + 512] for kt in range(NKT)]
    uh_tiles = [U2.ap()[:, kt * 1024 + 512:(kt + 1) * 1024] for kt in range(NKT)]
    gsh = U1
    gs_bf = p.sb((128, 512), BF16)
    sg_sb = p.sb((128, 512))
    up_sb = p.sb((128, 512))
    wsl_up = p.sb((128, 2048), BF16)
    wsl_gt = wslo
    wsl_dn = p.sb((128, 8192), BF16)
    for ch in range(NCH):
        cs = slice(ch * TC, (ch + 1) * TC)
        emit_ln(p, x_tiles, ch, ps_s, ps_q, ps_mb, ps_rb, ones128, sq_sb, st_sb)
        for kt in range(NKT):
            p.op("vector", lambda e, o=tsb.ap(), a=x_tiles[kt].ap()[:, cs],
                 b=ps_mb.ap():
                 e.tensor_tensor(o, a, b, AluOpType.subtract), inc=1)
            p.op("vector", lambda e, o=u_tiles[kt], i=tsb.ap(),
                 s=suug.ap()[:, kt:kt + 1], b=ps_rb.ap():
                 e.scalar_tensor_tensor(o, i, s, b, AluOpType.mult, AluOpType.mult),
                 inc=1)
            p.op("tensor", lambda e, o=ps_a.ap(), l=h128b.ap(), r=u_tiles[kt]:
                 e.matmul(o, l, r, start=True, stop=True), inc=1)
            p.op("scalar", lambda e, o=uh_tiles[kt], i=ps_a.ap():
                 e.copy(o, i), inc=1)
        for ot in range(NOT_I):
            p.dma_burst([(wsl_up.ap()[:, kt * 128:(kt + 1) * 128], wup4[ot, kt])
                         for kt in range(NKT)]
                        + [(wsl_gt.ap()[:, kt * 128:(kt + 1) * 128], wgate4[ot, kt])
                           for kt in range(NKT)])
            p.mm_group(ps_b.ap(), [(wsl_up.ap()[:, kt * 128:(kt + 1) * 128],
                                    uh_tiles[kt]) for kt in range(NKT)])
            p.mm_group(ps_c.ap(), [(wsl_gt.ap()[:, kt * 128:(kt + 1) * 128],
                                    uh_tiles[kt]) for kt in range(NKT)])
            p.op("vector", lambda e, o=up_sb.ap(), i=ps_b.ap(),
                 s=wsup.ap()[:, ot:ot + 1]:
                 e.tensor_scalar_mul(o, i, s), inc=1)
            p.op("scalar", lambda e, o=sg_sb.ap(), i=ps_c.ap(),
                 s=wsgt.ap()[:, ot:ot + 1]:
                 e.activation(o, i, AF.Silu, scale=s), inc=1)
            p.op("vector", lambda e, o=gs_bf.ap(), i=up_sb.ap(),
                 s=sudp.ap()[:, ot:ot + 1], b=sg_sb.ap():
                 e.scalar_tensor_tensor(o, i, s, b, AluOpType.mult, AluOpType.mult),
                 inc=1)
            p.op("tensor", lambda e, o=ps_d.ap(), l=h128b.ap(), r=gs_bf.ap():
                 e.matmul(o, l, r, start=True, stop=True), inc=1)
            p.op("scalar", lambda e, o=gsh.ap()[:, ot * 512:(ot + 1) * 512],
                 i=ps_d.ap(): e.copy(o, i), inc=1)
        for o2 in range(NKT):
            p.dma_burst([(wsl_dn.ap()[:, kt * 128:(kt + 1) * 128], wdn4[o2, kt])
                         for kt in range(NOT_I)])
            p.mm_group(ps_b.ap(), [(wsl_dn.ap()[:, kt * 128:(kt + 1) * 128],
                                    gsh.ap()[:, kt * 512:(kt + 1) * 512])
                                   for kt in range(NOT_I)])
            p.op("vector", lambda e, o=x_tiles[o2].ap()[:, cs], i=ps_b.ap(),
                 s=wsdp.ap()[:, o2:o2 + 1], b=x_tiles[o2].ap()[:, cs]:
                 e.scalar_tensor_tensor(o, i, s, b, AluOpType.mult, AluOpType.add),
                 inc=1)
        for o2 in range(NKT):
            p.dma(x2o[o2 * 128:(o2 + 1) * 128, cs], x_tiles[o2].ap()[:, cs])

    xn_bf = [U2.ap()[:, kt * 1024:(kt + 1) * 1024] for kt in range(NKT)]
    for ch in range(NCH):
        cs = slice(ch * TC, (ch + 1) * TC)
        emit_ln(p, x_tiles, ch, ps_s, ps_q, ps_mb, ps_rb, ones128, sq_sb, st_sb)
        for kt in range(NKT):
            p.op("vector", lambda e, o=tsb.ap(), a=x_tiles[kt].ap()[:, cs],
                 b=ps_mb.ap():
                 e.tensor_tensor(o, a, b, AluOpType.subtract), inc=1)
            p.op("vector", lambda e, o=xn_bf[kt][:, cs], a=tsb.ap(),
                 b=ps_rb.ap():
                 e.tensor_tensor(o, a, b, AluOpType.mult), inc=1)

    qf = tsb
    t1 = sg_sb
    t2 = up_sb
    for (w4, nots, od, do_rope) in ((wq14, 16, q1o, True), (wk14, 4, k1o, True),
                                    (wv14, 4, v1o, False)):
        for ot in range(nots):
            p.dma_burst([(wslo.ap()[:, kt * 128:(kt + 1) * 128], w4[ot, kt])
                         for kt in range(NKT)])
            for ch in range(NCH):
                cs = slice(ch * TC, (ch + 1) * TC)
                p.mm_group(ps_b.ap(), [(wslo.ap()[:, kt * 128:(kt + 1) * 128],
                                        xn_bf[kt][:, cs]) for kt in range(NKT)])
                if do_rope:
                    p.op("vector", lambda e, o=qf.ap(), i=ps_b.ap():
                         e.tensor_copy(o, i), inc=1)
                    p.op("tensor", lambda e, o=ps_c.ap(), l=bc(perm.ap()),
                         r=bc(qf.ap()):
                         e.matmul(o, l, r, start=True, stop=True), inc=1)
                    p.op("vector", lambda e, o=t1.ap(), a=qf.ap(),
                         b=cos_sb.ap()[:, cs]:
                         e.tensor_tensor(o, a, b, AluOpType.mult), inc=1)
                    p.op("vector", lambda e, o=t2.ap(), a=ps_c.ap(),
                         b=sin_sb.ap()[:, cs]:
                         e.tensor_tensor(o, a, b, AluOpType.mult), inc=1)
                    p.op("vector", lambda e, o=t1.ap(), a=t1.ap(), b=t2.ap():
                         e.tensor_tensor(o, a, b, AluOpType.add), inc=1)
                    p.dma(od[ot * 128:(ot + 1) * 128, cs], t1.ap())
                else:
                    p.op("vector", lambda e, o=qf.ap(), i=ps_b.ap():
                         e.tensor_copy(o, i), inc=1)
                    p.dma(od[ot * 128:(ot + 1) * 128, cs], qf.ap())
    return p.build()



def build_launch_d():
    p = P("ld")
    L = 4096
    qd = [p.din(f"q{u}", (64, L)) for u in range(8)]
    kd = [p.din(f"k{u}", (64, L)) for u in range(8)]
    vd = [p.din(f"v{u}", (L, 64)) for u in range(8)]
    maskd = p.din("maskd", (512, 512))
    ones128d = p.din("ones128", (128, 1))
    ones1d = p.din("ones1", (1, 128))
    co = [p.dout(f"c{u}", (64, L)) for u in range(8)]

    ones128 = p.sb((128, 1))
    ones1 = p.sb((1, 128))
    mask_sb = p.sb((128, 4 * 512))
    p.dma_burst([(ones128.ap(), ones128d), (ones1.ap(), ones1d)]
                + [(mask_sb.ap()[:, mi * 512:(mi + 1) * 512],
                    maskd[mi * 128:(mi + 1) * 128, :]) for mi in range(4)])
    q_sb = p.sb((64, L))
    k_sb = p.sb((64, L))
    v_sb = p.sb((128, 32 * 64))
    pt_sb = p.sb((128, 512))
    sm_sb = p.sb((128, 512))
    rs_sb = p.sb((1, 512))
    cx_sb = p.sb((64, 512))
    ps_S, ps_sum, ps_ctx, ps_bb = p.ps(), p.ps(), p.ps(), p.ps()

    for u in range(8):
        p.dma_burst([(q_sb.ap(), qd[u]), (k_sb.ap(), kd[u])]
                    + [(v_sb.ap()[:, jt * 64:(jt + 1) * 64],
                        vd[u][jt * 128:(jt + 1) * 128, :]) for jt in range(32)])
        for ic in range(8):
            ics = slice(ic * 512, (ic + 1) * 512)
            nj = 4 * ic + 4
            for jt in range(nj):
                p.op("tensor", lambda e, o=ps_S.ap(),
                     l=bc(k_sb.ap()[:, jt * 128:(jt + 1) * 128]),
                     r=bc(q_sb.ap()[:, ics]):
                     e.matmul(o, l, r, start=True, stop=True), inc=1)
                if jt >= 4 * ic:
                    mi = jt - 4 * ic
                    p.op("vector", lambda e, o=sm_sb.ap(), a=ps_S.ap(),
                         b=mask_sb.ap()[:, mi * 512:(mi + 1) * 512]:
                         e.tensor_tensor(o, a, b, AluOpType.add), inc=1)
                    p.op("scalar", lambda e, o=pt_sb.ap(), i=sm_sb.ap(),
                         s=p.fconst(L1_SM_SCALE):
                         e.activation(o, i, AF.Exp, scale=s), inc=1)
                else:
                    p.op("scalar", lambda e, o=pt_sb.ap(), i=ps_S.ap(),
                         s=p.fconst(L1_SM_SCALE):
                         e.activation(o, i, AF.Exp, scale=s), inc=1)
                p.op("tensor", lambda e, o=ps_sum.ap()[0:1, :],
                     l=bc(ones128.ap()), r=bc(pt_sb.ap()),
                     st=(jt == 0), sp=(jt == nj - 1):
                     e.matmul(o, l, r, start=st, stop=sp), inc=1)
                p.op("tensor", lambda e, o=ps_ctx.ap()[0:64, :],
                     l=bc(v_sb.ap()[:, jt * 64:(jt + 1) * 64]), r=bc(pt_sb.ap()),
                     st=(jt == 0), sp=(jt == nj - 1):
                     e.matmul(o, l, r, start=st, stop=sp), inc=1)
            p.op("vector", lambda e, o=rs_sb.ap(), i=ps_sum.ap()[0:1, :]:
                 e.reciprocal(o, i), inc=1)
            p.op("tensor", lambda e, o=ps_bb.ap(), l=bc(ones1.ap()), r=bc(rs_sb.ap()):
                 e.matmul(o, l, r, start=True, stop=True), inc=1)
            p.op("vector", lambda e, o=cx_sb.ap(), i=ps_ctx.ap()[0:64, :]:
                 e.tensor_copy(o, i), inc=1)
            p.op("vector", lambda e, o=cx_sb.ap(), a=cx_sb.ap(),
                 b=ps_bb.ap()[0:64, :]:
                 e.tensor_tensor(o, a, b, AluOpType.mult), inc=1)
            p.dma(co[u][:, ics], cx_sb.ap())
    return p.build()



def build_launch_e():
    p = P("le")
    x2t = p.din("x2t", (D, TPC))
    yt = p.din("yt", (D, TPC))
    wo14 = p.din("wo14", (16, 16, 128, 128), BF16)
    w14 = p.din("w14", (64, 16, 128, 128), BF16)
    w34 = p.din("w34", (64, 16, 128, 128), BF16)
    w24 = p.din("w24", (16, 64, 128, 128), BF16)
    h128bd = p.din("h128b", (128, 128), BF16)
    ones128d = p.din("ones128", (128, 1))
    ones1d = p.din("ones1", (1, 128))
    oo = p.dout("oo", (D, TPC))

    x_tiles = [p.sb((128, TPC)) for _ in range(NKT)]
    p.dma_burst([(x_tiles[kt].ap(), x2t[kt * 128:(kt + 1) * 128, :])
                 for kt in range(NKT)])
    ones128 = p.sb((128, 1))
    ones1 = p.sb((1, 128))
    h128b = p.sb((128, 128), BF16)
    p.dma_burst([(ones128.ap(), ones128d), (ones1.ap(), ones1d),
                 (h128b.ap(), h128bd)])
    emit_ln.ones1 = ones1

    ybf = [p.sb((128, TPC), BF16) for _ in range(NKT)]
    yscr = p.sb((128, 512))
    for kt in range(NKT):
        for ch in range(NCH):
            cs = slice(ch * TC, (ch + 1) * TC)
            p.dma(yscr.ap(), yt[kt * 128:(kt + 1) * 128, cs])
            p.op("scalar", lambda e, o=ybf[kt].ap()[:, cs], i=yscr.ap():
                 e.copy(o, i), inc=1)

    sq_sb = p.sb((128, 512))
    st_sb = tuple(p.sb((1, 512)) for _ in range(3))
    ps_s, ps_q, ps_mb, ps_rb = p.ps(), p.ps(), p.ps(), p.ps()
    ps_a, ps_b, ps_c, ps_d = p.ps(), p.ps(), p.ps(), p.ps()
    wslo = p.sb((128, 2048), BF16)

    for ot in range(NKT):
        p.dma_burst([(wslo.ap()[:, kt * 128:(kt + 1) * 128], wo14[ot, kt])
                     for kt in range(NKT)])
        for ch in range(NCH):
            cs = slice(ch * TC, (ch + 1) * TC)
            p.mm_group(ps_b.ap(), [(wslo.ap()[:, kt * 128:(kt + 1) * 128],
                                    ybf[kt].ap()[:, cs]) for kt in range(NKT)])
            p.op("vector", lambda e, o=x_tiles[ot].ap()[:, cs],
                 a=x_tiles[ot].ap()[:, cs], b=ps_b.ap():
                 e.tensor_tensor(o, a, b, AluOpType.add), inc=1)

    xn_bf = ybf
    gsh = p.sb((128, NOT_I * 512), BF16)
    gs_bf = p.sb((128, 512), BF16)
    sg_sb = p.sb((128, 512))
    tsb = p.sb((128, 512))
    wsl_g = p.sb((128, 2048), BF16)
    wsl_l = p.sb((128, 2048), BF16)
    wsl_dn = p.sb((128, 8192), BF16)
    for ch in range(NCH):
        cs = slice(ch * TC, (ch + 1) * TC)
        emit_ln(p, x_tiles, ch, ps_s, ps_q, ps_mb, ps_rb, ones128, sq_sb, st_sb)
        for kt in range(NKT):
            p.op("vector", lambda e, o=tsb.ap(), a=x_tiles[kt].ap()[:, cs],
                 b=ps_mb.ap():
                 e.tensor_tensor(o, a, b, AluOpType.subtract), inc=1)
            p.op("vector", lambda e, o=xn_bf[kt][:, cs], a=tsb.ap(),
                 b=ps_rb.ap():
                 e.tensor_tensor(o, a, b, AluOpType.mult), inc=1)
        for ot in range(NOT_I):
            p.dma_burst([(wsl_g.ap()[:, kt * 128:(kt + 1) * 128], w14[ot, kt])
                         for kt in range(NKT)]
                        + [(wsl_l.ap()[:, kt * 128:(kt + 1) * 128], w34[ot, kt])
                           for kt in range(NKT)])
            p.mm_group(ps_b.ap(), [(wsl_g.ap()[:, kt * 128:(kt + 1) * 128],
                                    xn_bf[kt][:, cs]) for kt in range(NKT)])
            p.mm_group(ps_c.ap(), [(wsl_l.ap()[:, kt * 128:(kt + 1) * 128],
                                    xn_bf[kt][:, cs]) for kt in range(NKT)])
            p.op("scalar", lambda e, o=sg_sb.ap(), i=ps_b.ap():
                 e.activation(o, i, AF.Silu), inc=1)
            p.op("vector", lambda e, o=gs_bf.ap(), i=ps_c.ap(), b=sg_sb.ap():
                 e.scalar_tensor_tensor(o, i, C3, b, AluOpType.mult, AluOpType.mult),
                 inc=1)
            p.op("tensor", lambda e, o=ps_d.ap(), l=h128b.ap(), r=gs_bf.ap():
                 e.matmul(o, l, r, start=True, stop=True), inc=1)
            p.op("scalar", lambda e, o=gsh.ap()[:, ot * 512:(ot + 1) * 512],
                 i=ps_d.ap(): e.copy(o, i), inc=1)
        for o2 in range(NKT):
            p.dma_burst([(wsl_dn.ap()[:, kt * 128:(kt + 1) * 128], w24[o2, kt])
                         for kt in range(NOT_I)])
            p.mm_group(ps_b.ap(), [(wsl_dn.ap()[:, kt * 128:(kt + 1) * 128],
                                    gsh.ap()[:, kt * 512:(kt + 1) * 512])
                                   for kt in range(NOT_I)])
            p.op("vector", lambda e, o=x_tiles[o2].ap()[:, cs],
                 a=x_tiles[o2].ap()[:, cs], b=ps_b.ap():
                 e.tensor_tensor(o, a, b, AluOpType.add), inc=1)
            p.dma(oo[o2 * 128:(o2 + 1) * 128, cs], x_tiles[o2].ap()[:, cs])
    return p.build()



def tile4(wt, nkt, not_):
    Din, Dout = wt.shape
    assert Din == nkt * 128 and Dout == not_ * 128
    return np.ascontiguousarray(
        wt.reshape(nkt, 128, not_, 128).transpose(2, 0, 1, 3))


def shard_cols(a, n=NCORES):
    c = a.shape[1] // n
    return [np.ascontiguousarray(a[:, i * c:(i + 1) * c]) for i in range(n)]


_CACHE = {}
DEBUG = False
REC = {}


def _rec(**kw):
    if DEBUG:
        REC.update(kw)


def _log(msg):
    import sys
    print(f"[kernel] {msg}", file=sys.stderr, flush=True)


def kernel(**inputs):
    f32 = np.float32
    x = np.ascontiguousarray(inputs["L0_attn_x"], dtype=f32)

    H128 = _had(128)
    H16 = _had(16)
    H64 = _had(64)
    ones128 = np.ones((128, 1), f32)
    ones1 = np.ones((1, 128), f32)
    permm = np.zeros((128, 128), f32)
    for i in range(128):
        permm[i ^ 32, i] = 1.0
    invf = (1.0 / 10000.0 ** (np.arange(32, dtype=np.float64) * 2.0 / 64.0))
    ang = np.arange(4096, dtype=np.float64)[:, None] * invf[None, :]
    cosb = np.cos(ang).T.astype(f32)
    sinb = np.sin(ang).T.astype(f32)
    rr = np.arange(128)
    cos2 = cosb[rr % 32, :]
    sgn = np.where((rr % 64) < 32, -1.0, 1.0).astype(f32)
    sin2 = sinb[rr % 32, :] * sgn[:, None]
    jj = np.arange(128)[:, None]
    ii = np.arange(512)[None, :]
    masks = np.concatenate(
        [np.where(mi * 128 + jj <= ii, 0.0, NEG).astype(f32) for mi in range(4)],
        axis=0)

    WupT = np.ascontiguousarray(inputs["L0_ffn_W_up"].astype(f32).T)
    WgateT = np.ascontiguousarray(inputs["L0_ffn_W_gate"].astype(f32).T)
    WdnT = np.ascontiguousarray(inputs["L0_ffn_W_down"].astype(f32).T)
    W2T = np.ascontiguousarray(inputs["L1_ffn_W2"].astype(f32).T)
    wup_v = WupT.reshape(16, 131072 * 8)
    wgate_v = WgateT.reshape(16, 131072 * 8)
    wdn_v = WdnT.reshape(64, 32768 * 8)
    w2_v = W2T.reshape(64, 32768 * 8)

    conv_list = [
        ("wo0", np.ascontiguousarray(inputs["L0_attn_Wo"].astype(f32).T)),
        ("wq1", np.ascontiguousarray(inputs["L1_attn_Wq"].astype(f32))),
        ("wk1", np.ascontiguousarray(inputs["L1_attn_Wk"].astype(f32))),
        ("wv1", np.ascontiguousarray(inputs["L1_attn_Wv"].astype(f32))),
        ("wo1", np.ascontiguousarray(inputs["L1_attn_Wo"].astype(f32))),
        ("w1t", np.ascontiguousarray(inputs["L1_ffn_W1"].astype(f32).T)),
        ("w3t", np.ascontiguousarray(inputs["L1_ffn_W3"].astype(f32).T)),
        ("bo_lo", np.ascontiguousarray(inputs["L0_attn_Bo"].astype(f32).T)),
        ("ao_lo", np.ascontiguousarray(inputs["L0_attn_Ao"].astype(f32).T)),
    ]
    flat = np.concatenate([a.ravel() for _, a in conv_list])
    tot = flat.size
    percore = -(-tot // (NCORES * 128 * 512)) * 128 * 512
    padded = np.zeros(NCORES * percore, f32)
    padded[:tot] = flat
    conv_cols = percore // 128
    conv_shards = padded.reshape(NCORES, 128, conv_cols)

    if "W" not in _CACHE:
        _CACHE["W"] = build_launch_w(conv_cols)
        _CACHE["W_cols"] = conv_cols
    elif _CACHE["W_cols"] != conv_cols:
        _CACHE["W"] = build_launch_w(conv_cols)
        _CACHE["W_cols"] = conv_cols
    in_maps = []
    for c in range(NCORES):
        in_maps.append({
            "wup": np.ascontiguousarray(wup_v[:, c * 131072:(c + 1) * 131072]),
            "wgate": np.ascontiguousarray(wgate_v[:, c * 131072:(c + 1) * 131072]),
            "wdn": np.ascontiguousarray(wdn_v[:, c * 32768:(c + 1) * 32768]),
            "w2": np.ascontiguousarray(w2_v[:, c * 32768:(c + 1) * 32768]),
            "h16": H16, "h64": H64,
            "cin": np.ascontiguousarray(conv_shards[c]),
        })
    res = run_bass_kernel_spmd(_CACHE["W"], in_maps, core_ids=list(range(NCORES)))
    rw = res.results
    wup_b = np.concatenate([rw[c]["oup"] for c in range(NCORES)], axis=1)

    wgate_b = np.concatenate([rw[c]["ogate"] for c in range(NCORES)], axis=1)
    wdn_b = np.concatenate([rw[c]["odn"] for c in range(NCORES)], axis=1)
    w2_b = np.concatenate([rw[c]["o2"] for c in range(NCORES)], axis=1)
    Wup2 = wup_b.reshape(2048, 8192)
    _rec(Wup2=Wup2)
    Wgate2 = wgate_b.reshape(2048, 8192)
    Wdn2 = wdn_b.reshape(8192, 2048)
    W22 = w2_b.reshape(8192, 2048)
    _log("launch W done")
    conv_flat = np.concatenate([rw[c]["cout"].ravel() for c in range(NCORES)])
    convd = {}
    off = 0
    for nm, a in conv_list:
        convd[nm] = conv_flat[off:off + a.size].reshape(a.shape)
        off += a.size

    xT = np.ascontiguousarray(x.reshape(NT, D).T)
    xT_sh = shard_cols(xT)
    if "A" not in _CACHE:
        _CACHE["A"] = build_launch_a()
    in_maps = []
    wqkv = {}
    for nm, W, b, B, A in (("q", "Wq", "bq", "Bq", "Aq"),
                           ("k", "Wk", "bk", "Bk", "Ak"),
                           ("v", "Wv", "bv", "Bv", "Av")):
        W_ = inputs[f"L0_attn_{W}"].astype(f32)
        wqkv[f"w4{nm}"] = tile4(np.ascontiguousarray(W_.T), 16, 16)
        wqkv[f"bt{nm}"] = np.ascontiguousarray(
            inputs[f"L0_attn_{b}"].astype(f32).reshape(16, 128).T)
        wqkv[f"blo{nm}"] = np.ascontiguousarray(
            inputs[f"L0_attn_{B}"].astype(f32).T.reshape(16, 128, 16))
        wqkv[f"a4{nm}"] = np.ascontiguousarray(
            inputs[f"L0_attn_{A}"].astype(f32).T.reshape(16, 16, 128)
            .transpose(1, 0, 2))
    for c in range(NCORES):
        m = {"xt": xT_sh[c], "ones128": ones128, "ones1": ones1}
        m.update(wqkv)
        in_maps.append(m)
    res = run_bass_kernel_spmd(_CACHE["A"], in_maps, core_ids=list(range(NCORES)))
    ra = res.results
    qT = np.concatenate([ra[c]["oq"] for c in range(NCORES)], axis=1)
    kT = np.concatenate([ra[c]["ok"] for c in range(NCORES)], axis=1)
    vT = np.concatenate([ra[c]["ov"] for c in range(NCORES)], axis=1)
    _log("launch A done")
    _rec(qT=qT, kT=kT, vT=vT)

    if "B" not in _CACHE:
        _CACHE["B"] = build_launch_b()
    v_tok = np.ascontiguousarray(vT.T)
    in_maps = []
    for c in range(NCORES):
        m = {"ones128": ones128, "ones1": ones1}
        for u in range(4):
            g = 4 * c + u
            b_, h_ = g // 16, g % 16
            m[f"q{u}"] = np.ascontiguousarray(
                qT[h_ * 128:(h_ + 1) * 128, b_ * 4096:(b_ + 1) * 4096])
            m[f"k{u}"] = np.ascontiguousarray(
                kT[h_ * 128:(h_ + 1) * 128, b_ * 4096:(b_ + 1) * 4096])
            m[f"v{u}"] = np.ascontiguousarray(
                v_tok[b_ * 4096:(b_ + 1) * 4096, h_ * 128:(h_ + 1) * 128])
        in_maps.append(m)
    res = run_bass_kernel_spmd(_CACHE["B"], in_maps, core_ids=list(range(NCORES)))
    rb = res.results
    ctxT = np.zeros((D, NT), BF)
    for c in range(NCORES):
        for u in range(4):
            g = 4 * c + u
            b_, h_ = g // 16, g % 16
            ctxT[h_ * 128:(h_ + 1) * 128, b_ * 4096:(b_ + 1) * 4096] = rb[c][f"c{u}"]

    _log("launch B done")
    _rec(ctxT=ctxT)
    if "C" not in _CACHE:
        _CACHE["C"] = build_launch_c()
    ctx_sh = shard_cols(ctxT)
    wo4 = tile4(convd["wo0"].astype(BF), 16, 16)
    bot = np.ascontiguousarray(inputs["L0_attn_bo"].astype(f32).reshape(16, 128).T)
    bolo = np.ascontiguousarray(convd["bo_lo"].astype(BF).reshape(16, 128, 16))
    ao4 = np.ascontiguousarray(
        convd["ao_lo"].astype(BF).reshape(16, 16, 128).transpose(1, 0, 2))
    su_ug_t = np.ascontiguousarray(
        inputs["L0_ffn_SU_ug"].astype(f32).reshape(16, 128).T)
    su_dp_t = np.ascontiguousarray(
        inputs["L0_ffn_SU_dp"].astype(f32).reshape(64, 128).T)
    ws = inputs["L0_ffn_Wscale_ug"].astype(f32)
    ws_up_t = np.ascontiguousarray(ws[:INTER].reshape(64, 128).T)
    ws_gate_t = np.ascontiguousarray(ws[INTER:].reshape(64, 128).T)
    ws_dp_t = np.ascontiguousarray(
        inputs["L0_ffn_Wscale_dp"].astype(f32).reshape(16, 128).T)
    wup4 = tile4(Wup2, 16, 64)
    wgate4 = tile4(Wgate2, 16, 64)
    wdn4 = tile4(Wdn2, 64, 16)
    wq14 = tile4(convd["wq1"].astype(BF), 16, 16)
    wk14 = tile4(convd["wk1"].astype(BF), 16, 4)
    wv14 = tile4(convd["wv1"].astype(BF), 16, 4)
    h128b = H128.astype(BF)
    in_maps = []
    for c in range(NCORES):
        pos0 = (c % 4) * 1024
        m = {"xt": xT_sh[c], "ctxt": ctx_sh[c], "wo4": wo4, "bot": bot,
             "bolo": bolo, "ao4": ao4, "su_ug": su_ug_t, "su_dp": su_dp_t,
             "ws_up": ws_up_t, "ws_gate": ws_gate_t, "ws_dp": ws_dp_t,
             "wup4": wup4, "wgate4": wgate4, "wdn4": wdn4, "h128b": h128b,
             "ones128": ones128, "ones1": ones1, "wq14": wq14, "wk14": wk14,
             "wv14": wv14,
             "cosd": np.ascontiguousarray(cos2[:, pos0:pos0 + 1024]),
             "sind": np.ascontiguousarray(sin2[:, pos0:pos0 + 1024]),
             "permd": permm}
        in_maps.append(m)
    res = run_bass_kernel_spmd(_CACHE["C"], in_maps, core_ids=list(range(NCORES)))
    rc = res.results
    x2T = np.concatenate([rc[c]["x2o"] for c in range(NCORES)], axis=1)
    q1T = np.concatenate([rc[c]["q1o"] for c in range(NCORES)], axis=1)
    k1T = np.concatenate([rc[c]["k1o"] for c in range(NCORES)], axis=1)
    v1T = np.concatenate([rc[c]["v1o"] for c in range(NCORES)], axis=1)
    _log("launch C done")
    _rec(x2T=x2T, q1T=q1T, k1T=k1T, v1T=v1T)

    if "D" not in _CACHE:
        _CACHE["D"] = build_launch_d()
    v1_tok = np.ascontiguousarray(v1T.T)
    in_maps = []
    for c in range(NCORES):
        m = {"maskd": masks, "ones128": ones128, "ones1": ones1}
        for u in range(8):
            g = 8 * c + u
            b_, h_ = g // 32, g % 32
            gg = h_ // 4
            m[f"q{u}"] = np.ascontiguousarray(
                q1T[h_ * 64:(h_ + 1) * 64, b_ * 4096:(b_ + 1) * 4096])
            m[f"k{u}"] = np.ascontiguousarray(
                k1T[gg * 64:(gg + 1) * 64, b_ * 4096:(b_ + 1) * 4096])
            m[f"v{u}"] = np.ascontiguousarray(
                v1_tok[b_ * 4096:(b_ + 1) * 4096, gg * 64:(gg + 1) * 64])
        in_maps.append(m)
    res = run_bass_kernel_spmd(_CACHE["D"], in_maps, core_ids=list(range(NCORES)))
    rd = res.results
    ctx1T = np.zeros((D, NT), f32)
    for c in range(NCORES):
        for u in range(8):
            g = 8 * c + u
            b_, h_ = g // 32, g % 32
            ctx1T[h_ * 64:(h_ + 1) * 64, b_ * 4096:(b_ + 1) * 4096] = rd[c][f"c{u}"]

    _log("launch D done")
    _rec(ctx1T=ctx1T)
    yT_parts = []
    for b_ in range(2):
        yb = ctx1T[:, b_ * 4096:(b_ + 1) * 4096].reshape(2048, 2, 2048)
        y_b = yb.reshape(4096, 2048)
        yT_parts.append(y_b.T)
    yT = np.ascontiguousarray(np.concatenate(yT_parts, axis=1))

    if "E" not in _CACHE:
        _CACHE["E"] = build_launch_e()
    x2_sh = shard_cols(x2T)
    y_sh = shard_cols(yT)
    wo14 = tile4(convd["wo1"].astype(BF), 16, 16)
    w14 = tile4(convd["w1t"].astype(BF), 16, 64)
    w34 = tile4(convd["w3t"].astype(BF), 16, 64)
    w24 = tile4(W22, 64, 16)
    in_maps = []
    for c in range(NCORES):
        m = {"x2t": x2_sh[c], "yt": y_sh[c], "wo14": wo14, "w14": w14,
             "w34": w34, "w24": w24, "h128b": h128b, "ones128": ones128,
             "ones1": ones1}
        in_maps.append(m)
    res = run_bass_kernel_spmd(_CACHE["E"], in_maps, core_ids=list(range(NCORES)))
    re_ = res.results
    outT = np.concatenate([re_[c]["oo"] for c in range(NCORES)], axis=1)
    _log("launch E done")
    return np.ascontiguousarray(outT.T).reshape(2, 4096, 2048).astype(np.float32)
